# version 21
# baseline (speedup 1.0000x reference)
import contextlib

import numpy as np
import ml_dtypes

import concourse.bass as bass
import concourse.mybir as mybir
from concourse.bass_utils import run_bass_kernel_spmd

F32 = mybir.dt.float32
BF16 = mybir.dt.bfloat16
AF = mybir.ActivationFunctionType
ALU = mybir.AluOpType
AX = mybir.AxisListType

S = 4096
H = 1024
NCH = 8
SB = 512
NSB = S // SB
IB = 256
NIB = S // IB
NJ = S // 128
CSHIFT = 20.0
SCALE = 1.0 / 32.0
XST = 528
XOFF = 16

ENGINES = ("sync", "gpsimd", "vector", "scalar", "tensor")
SEMS = ENGINES + ("scalardma",)


class Op:
    __slots__ = ("engine", "fn", "deps", "event", "dma", "drain_after", "idx", "sem")

    def __init__(self, engine, fn, deps, event, dma, drain_after):
        self.engine = engine
        self.fn = fn
        self.deps = list(deps)
        self.event = event
        self.dma = dma
        self.drain_after = drain_after


class Sched:

    def __init__(self):
        self.ops = []
        self.ev_engine = {}

    def add(self, engine, fn, deps=(), event=None, dma=False, drain_after=False):
        op = Op(engine, fn, deps, event, dma, drain_after)
        op.sem = "scalardma" if (dma and engine == "scalar") else engine
        op.idx = len(self.ops)
        self.ops.append(op)
        if event is not None:
            assert event not in self.ev_engine, f"dup event {event}"
            self.ev_engine[event] = (op.sem, op.idx)
        return op

    def resolve(self):
        referenced = set()
        for op in self.ops:
            for d in op.deps:
                assert d in self.ev_engine, f"unknown dep {d}"
                referenced.add(d)
        counts = {e: 0 for e in SEMS}
        self.ev_val = {}
        self.inc_amt = {}
        for op in self.ops:
            if op.dma:
                amt = 16
            elif op.event is not None and op.event in referenced:
                amt = 1
            else:
                amt = 0
            counts[op.sem] += amt
            self.inc_amt[op.idx] = amt
            if op.event is not None:
                self.ev_val[op.event] = (op.sem, counts[op.sem])
        for op in self.ops:
            for d in op.deps:
                de, di = self.ev_engine[d]
                if de == op.engine:
                    assert di < op.idx, f"same-engine forward dep {d}"

    def emit(self, engine, eng, sems):
        last_wait = {}
        for op in self.ops:
            if op.engine != engine:
                continue
            for d in op.deps:
                dsem, _ = self.ev_engine[d]
                if dsem == engine:
                    continue
                val = self.ev_val[d][1]
                if last_wait.get(dsem, -1) >= val:
                    continue
                eng.wait_ge(sems[dsem], val)
                last_wait[dsem] = val
            instr = op.fn(eng)
            amt = self.inc_amt[op.idx]
            if amt:
                assert instr is not None, f"op {op.event} returned no instr"
                instr.then_inc(sems[op.sem], amt)
            if op.drain_after:
                eng.drain()


def build_program(debug=False):
    nc = bass.Bass("TRN2", target_bir_lowering=False, detect_race_conditions=False)

    dx = nc.dram_tensor("x", [S, H], F32, kind="ExternalInput").ap()
    dWq = nc.dram_tensor("WqT", [H, H], F32, kind="ExternalInput").ap()
    dWk = nc.dram_tensor("WkT", [H, H], F32, kind="ExternalInput").ap()
    dWv = nc.dram_tensor("WvT", [H, H], F32, kind="ExternalInput").ap()
    dWo = nc.dram_tensor("WoT", [H, H], F32, kind="ExternalInput").ap()
    dWr = nc.dram_tensor("WrT", [H, H], F32, kind="ExternalInput").ap()
    dWkf = nc.dram_tensor("WkfT", [H, H], F32, kind="ExternalInput").ap()
    dWvf = nc.dram_tensor("WvfT", [H, H], F32, kind="ExternalInput").ap()
    dmur = nc.dram_tensor("mur", [128, NCH], F32, kind="ExternalInput").ap()
    dmuk = nc.dram_tensor("muk", [128, NCH], F32, kind="ExternalInput").ap()
    dident = nc.dram_tensor("ident", [128, 128], F32, kind="ExternalInput").ap()
    dout = nc.dram_tensor("out", [S, H], F32, kind="ExternalOutput").ap()
    skind = "ExternalOutput" if debug else "Internal"
    dxn = nc.dram_tensor("xn_nat", [S, H], BF16, kind=skind).ap()
    dxnT = nc.dram_tensor("xnT_lin", [NSB, 128, NCH, SB], BF16, kind=skind).ap()
    dqT = nc.dram_tensor("qT_lin", [NIB, 128, NCH, IB], BF16, kind=skind).ap()
    dpart = nc.dram_tensor("part", [S, H], F32, kind=skind).ap()

    cursor = [16512]

    def alloc(name, shape, dtype, at=None):
        if at is not None:
            cursor[0] = at
        off = (cursor[0] + 31) // 32 * 32
        nbytes = int(np.prod(shape[1:])) * mybir.dt.size(dtype)
        t = nc.alloc_sbuf_tensor_at(name, list(shape), dtype, offset=off)
        cursor[0] = off + nbytes
        return t

    lnsum = alloc("lnsum", [128, 4], F32)
    lnsumsq = alloc("lnsumsq", [128, 4], F32)
    lnm = alloc("lnm", [128, 4], F32)
    lnex2 = alloc("lnex2", [128, 4], F32)
    lnt0 = alloc("lnt0", [128, 4], F32)
    lnt1 = alloc("lnt1", [128, 4], F32)
    lnt2 = alloc("lnt2", [128, 4], F32)
    lnr0 = alloc("lnr0", [128, 4], F32)
    lnrstd = alloc("lnrstd", [128, 4], F32)
    lnbias = alloc("lnbias", [128, 4], F32)
    mur = alloc("mur_sb", [128, NCH], F32)
    muk = alloc("muk_sb", [128, NCH], F32)
    ones_bf = alloc("ones_bf", [128, 1], BF16)
    ones_f = alloc("ones_f", [128, 1], F32)
    one_f = alloc("one_f", [1, 8], F32)
    negc = alloc("negc", [128, 1], F32)
    rs_f = alloc("rs_f", [1, IB], F32)
    recip = alloc("recip", [128, 2], F32)
    ident = alloc("ident_sb", [128, 128], F32)
    xnT = alloc("xnT", [128, 2, NCH, XST], BF16)
    W1 = alloc("W1", [128, NCH, H], BF16)
    FIXED_END = cursor[0]

    W2 = alloc("W2", [128, NCH, H], BF16)
    W3 = alloc("W3", [128, NCH, H], BF16)
    W4 = alloc("W4", [128, NCH, H], BF16)
    x_sb = alloc("x_sb", [128, 2, 4, H], F32)
    xn_sb = alloc("xn_sb", [128, 4, H], BF16)
    junk = alloc("junk", [128, H], F32)
    relu_t = alloc("relu_t", [128, 2, 512], F32)
    d_sb = alloc("d_sb", [128, NCH, SB], BF16)
    xr_sb = alloc("xr_sb", [128, NCH, SB], BF16)
    xk_sb = alloc("xk_sb", [128, NCH, SB], BF16)
    kact = alloc("kact", [128, NCH, SB], BF16)
    r_sb = alloc("r_sb", [128, 2, H], F32)
    fftmp = alloc("fftmp", [128, H], F32)
    qT_sb = alloc("qT_sb", [128, NCH, SB], BF16)
    part_sb = alloc("part_sb", [128, 2, H], F32)
    A_END = cursor[0]
    WvC = alloc("WvC", [128, NCH, H], BF16)
    rs4 = alloc("rs4", [128, IB], F32)
    assert cursor[0] <= 229376, cursor[0]

    kT = alloc("kT", [128, NCH, S], BF16, at=FIXED_END)
    v_sb = alloc("v_sb", [128, NJ, H], BF16)
    BC_BASE = cursor[0]
    PT = alloc("PT", [128, NJ, IB], BF16, at=BC_BASE)
    qTd = alloc("qTd", [128, 2, NCH, IB], BF16)
    partd = alloc("partd", [128, 2, H], F32)
    ctxT = alloc("ctxT", [128, NCH, IB], BF16)
    D_END = cursor[0]
    assert D_END <= 229376, D_END

    st = contextlib.ExitStack()
    pb = [st.enter_context(nc.psum_tensor(f"pb{i}", [128, 512], F32)) for i in range(8)]
    sems = {e: st.enter_context(nc.semaphore(f"sem_{e}")) for e in SEMS}
    block = st.enter_context(nc.Block())

    sc = Sched()
    bank_free = {i: None for i in range(8)}

    def wait_ev(e, ev):
        s, v = sc.ev_val[ev]
        e.wait_ge(sems[s], v)
    _abank = [0]

    def nb():
        b = _abank[0] % 8
        _abank[0] += 1
        return b

    def bank_dep(i):
        e = bank_free[i]
        return [e] if e else []

    AL = NSB - 1

    def xnT_done(j):
        if j == 0:
            return ["VxnT0_7"]
        return [f"LxnT{j}_{NCH-1}"]

    def F_vec_build(j):
        if j >= 1:
            sc.add("vector",
                   lambda e, j=j: e.tensor_copy(
                       xnT[:, j % 2, :, 15:16], xnT[:, (j - 1) % 2, :, XST - 1:XST]),
                   deps=xnT_done(j) + xnT_done(j - 1),
                   event=f"Vprev{j}", drain_after=True)
            dprev = [f"Vprev{j}"]
        else:
            dprev = xnT_done(j) + ["Lmur", "Lmuk"]
        sc.add("vector",
               lambda e, j=j: e.tensor_sub(
                   d_sb[:, :, :], xnT[:, j % 2, :, XOFF:XOFF + SB],
                   xnT[:, j % 2, :, XOFF - 1:XOFF - 1 + SB]),
               deps=dprev, drain_after=True)
        for cc in range(NCH):
            sc.add("vector",
                   lambda e, j=j, cc=cc: e.scalar_tensor_tensor(
                       xr_sb[:, cc, :], d_sb[:, cc, :], mur[:, cc:cc + 1],
                       xnT[:, j % 2, cc, XOFF - 1:XOFF - 1 + SB], ALU.mult, ALU.add),
                   event=f"Vxr{j}" if cc == NCH - 1 else None)
        for cc in range(NCH):
            sc.add("vector",
                   lambda e, j=j, cc=cc: e.scalar_tensor_tensor(
                       xk_sb[:, cc, :], d_sb[:, cc, :], muk[:, cc:cc + 1],
                       xnT[:, j % 2, cc, XOFF - 1:XOFF - 1 + SB], ALU.mult, ALU.add),
                   event=f"Vxk{j}" if cc == NCH - 1 else None)

    def F_qT(j):
        for cc in range(NCH):
            b = nb()

            def qgrp(e, j=j, cc=cc, b=b):
                for hc in range(NCH):
                    ins = e.matmul(pb[b][:, 0:512],
                                   W1[:, hc, cc * 128:(cc + 1) * 128],
                                   xnT[:, j % 2, hc, XOFF:XOFF + SB],
                                   start=(hc == 0), stop=(hc == NCH - 1))
                return ins

            deps = bank_dep(b)
            if cc == 0:
                deps += xnT_done(j)
                if j == 0:
                    deps += ["GWq"]
            sc.add("tensor", qgrp, deps=deps, event=f"Tq{j}_{cc}")
            sdeps = [f"Tq{j}_{cc}"]
            if cc == 0 and j >= 1:
                sdeps += [f"GqT{j-1}_1"]
            sc.add("scalar",
                   lambda e, cc=cc, b=b: e.activation(
                       qT_sb[:, cc, :], pb[b][:, 0:512], AF.Copy),
                   deps=sdeps, event=f"Sqev{j}_{cc}")
            bank_free[b] = f"Sqev{j}_{cc}"

    def LN_stats_s(j):
        for sub in range(4):
            deps = [f"Lx{j}"]
            if sub == 0 and j >= 1:
                deps += [f"VmathA{j-1}"]
            if sub == 0 and j == 1:
                deps += ["TxnT0_7"]
            sc.add("scalar",
                   lambda e, j=j, sub=sub: e.activation(
                       junk[:, :], x_sb[:, j % 2, sub, :], AF.Square,
                       accum_out=lnsumsq[:, sub:sub + 1]),
                   deps=deps,
                   event=f"Ssumsq{j}" if sub == 3 else None)

    def LN_stats_v(j):
        for sub in range(4):
            sc.add("vector",
                   lambda e, j=j, sub=sub: e.reduce_sum(
                       lnsum[:, sub:sub + 1], x_sb[:, j % 2, sub, :], axis=AX.X),
                   deps=[f"Lx{j}"], drain_after=(sub == 3))

    def LN_math(j):
        def mathA(e):
            e.tensor_scalar_mul(lnm[:, :], lnsum[:, :], 1.0 / H)
            e.tensor_scalar_mul(lnex2[:, :], lnsumsq[:, :], 1.0 / H)
            e.drain()
            e.tensor_mul(lnt0[:, :], lnm[:, :], lnm[:, :])
            e.drain()
            e.tensor_sub(lnt1[:, :], lnex2[:, :], lnt0[:, :])
            e.drain()
            e.tensor_scalar_add(lnt1[:, :], lnt1[:, :], 1e-5)
            e.drain()
            return e.reciprocal(lnt0[:, :], lnt1[:, :])

        sc.add("vector", mathA, deps=[f"Ssumsq{j}"], event=f"VmathA{j}")
        sc.add("scalar", lambda e: e.activation(lnr0[:, :], lnt0[:, :], AF.Sqrt),
               deps=[f"VmathA{j}"], event=f"Ssqrt{j}")

        def mathB(e):
            e.tensor_mul(lnt2[:, :], lnr0[:, :], lnr0[:, :])
            e.drain()
            e.tensor_mul(lnt2[:, :], lnt2[:, :], lnt1[:, :])
            e.drain()
            e.tensor_scalar(lnt2[:, :], lnt2[:, :], -0.5, 1.5, ALU.mult, ALU.add)
            e.drain()
            e.tensor_mul(lnrstd[:, :], lnr0[:, :], lnt2[:, :])
            e.drain()
            return e.scalar_tensor_tensor(
                lnbias[:, :], lnm[:, :], -1.0, lnrstd[:, :], ALU.mult, ALU.mult)

        sc.add("vector", mathB,
               deps=[f"Ssqrt{j}"] + ([f"Sxnev{j-1}"] if j >= 1 else []),
               event=f"VmathB{j}")

    def F_kpre(j, evac_fn=None):
        for dc in range(NCH):
            if dc == 4 and evac_fn is not None:
                evac_fn()
            b = nb()

            def kgrp(e, j=j, dc=dc, b=b):
                for cc in range(NCH):
                    ins = e.matmul(pb[b][:, 0:512],
                                   W3[:, cc, dc * 128:(dc + 1) * 128],
                                   xk_sb[:, cc, :],
                                   start=(cc == 0), stop=(cc == NCH - 1))
                return ins

            deps = bank_dep(b)
            if dc == 0:
                deps += [f"Vxk{j}"]
                if j == 0:
                    deps += ["GWkf"]
            sc.add("tensor", kgrp, deps=deps, event=f"Tkp{j}_{dc}")
            sdeps = [f"Tkp{j}_{dc}"]
            if dc >= 2:
                sdeps += [f"Vka{j}_{dc-2}"]
            elif j >= 1:
                sdeps += [f"Vka{j-1}_{dc+6}"]
            sc.add("scalar",
                   lambda e, dc=dc, b=b: e.activation(
                       relu_t[:, dc % 2, :], pb[b][:, 0:512], AF.Relu),
                   deps=sdeps, event=f"Srelu{j}_{dc}")
            bank_free[b] = f"Srelu{j}_{dc}"
            sc.add("vector",
                   lambda e, dc=dc: e.tensor_mul(
                       kact[:, dc, :], relu_t[:, dc % 2, :], relu_t[:, dc % 2, :]),
                   deps=[f"Srelu{j}_{dc}"], event=f"Vka{j}_{dc}")

    def LN_evac_scalar(j):
        for sub in range(4):
            deps = [f"VmathB{j}"]
            if sub == 0 and j >= 2:
                deps += [f"Gxnst{j-1}"]
            if j == 0:
                dst = junk if sub % 2 == 0 else fftmp
                if sub >= 2:
                    deps += [f"TxnT0_{(sub - 2) * 2 + 1}"]
                sc.add("scalar",
                       lambda e, sub=sub, dst=dst: e.activation(
                           dst[:, :], x_sb[:, 0, sub, :], AF.Identity,
                           bias=lnbias[:, sub:sub + 1], scale=lnrstd[:, sub:sub + 1]),
                       deps=deps,
                       event="Sxnev0" if sub == 3 else f"Sxnev0s{sub}")
            else:
                sc.add("scalar",
                       lambda e, j=j, sub=sub: e.activation(
                           xn_sb[:, sub, :], x_sb[:, j % 2, sub, :], AF.Identity,
                           bias=lnbias[:, sub:sub + 1], scale=lnrstd[:, sub:sub + 1]),
                       deps=deps, event=f"Sxnev{j}" if sub == 3 else None)

    def LN_evac(j):
        if j == 0:
            LN_evac_scalar(0)
            for q in range(8):
                sub, half = q // 2, q % 2
                src_t = junk if sub % 2 == 0 else fftmp
                b = 6 + (q % 2)

                def tgrp(e, sub=sub, half=half, src_t=src_t, b=b):
                    for k in range(4):
                        cc = half * 4 + k
                        ins = e.matmul(pb[b][:, k * 128:(k + 1) * 128],
                                       src_t[:, cc * 128:(cc + 1) * 128],
                                       ident[:, :],
                                       start=(k == 0), stop=(k == 3),
                                       is_transpose=True)
                    return ins

                tdeps = [f"Sxnev0s{sub}" if sub < 3 else "Sxnev0"]
                if q == 0:
                    tdeps += ["Lident"]
                if q >= 2:
                    tdeps += [f"VxnT0_{q-2}"]
                sc.add("tensor", tgrp, deps=tdeps, event=f"TxnT0_{q}")

                def tevac(e, sub=sub, half=half, b=b):
                    for k in range(4):
                        cc = half * 4 + k
                        ins = e.tensor_copy(
                            xnT[:, 0, cc, XOFF + sub * 128:XOFF + (sub + 1) * 128],
                            pb[b][:, k * 128:(k + 1) * 128])
                    return ins

                sc.add("vector", tevac, deps=[f"TxnT0_{q}"], event=f"VxnT0_{q}")
            bank_free[6] = "VxnT0_6"
            bank_free[7] = "VxnT0_7"
            return
        sc.add("gpsimd",
               lambda e, j=j: e.dma_start(
                   out=dxn[j * SB:(j + 1) * SB, :].rearrange("(a p) h -> p a h", p=128),
                   in_=xn_sb[:, :, :]),
               deps=[f"Sxnev{j}"], event=f"Gxnst{j}", dma=True)
        for cc in range(NCH):
            deps = [f"Gxnst{j}"]
            if j >= 2 and cc == 0:
                deps += [f"Vxk{j-2}", f"Tq{j-2}_{NCH-1}", f"Gxntl{j-2}"]
                if j - 1 >= 1:
                    deps += [f"Vprev{j-1}"]
            sc.add("sync",
                   lambda e, j=j, cc=cc: e.dma_start_transpose(
                       xnT[:, j % 2, cc, XOFF:XOFF + SB],
                       dxn[j * SB:(j + 1) * SB, cc * 128:(cc + 1) * 128]),
                   deps=deps, event=f"LxnT{j}_{cc}", dma=True)

    def F_rv(j):
        for ts in range(4):
            for dt in range(2):
                b = nb()

                def rgrp(e, j=j, ts=ts, dt=dt, b=b):
                    for cc in range(NCH):
                        ins = e.matmul(pb[b][:, 0:512],
                                       xr_sb[:, cc, ts * 128:(ts + 1) * 128],
                                       W2[:, cc, dt * 512:(dt + 1) * 512],
                                       start=(cc == 0), stop=(cc == NCH - 1))
                    return ins

                deps = bank_dep(b)
                if ts == 0 and dt == 0:
                    deps += [f"Vxr{j}"]
                    if j == 0:
                        deps += ["GWr"]
                sc.add("tensor", rgrp, deps=deps, event=f"Tr{j}_{ts}_{dt}")
                sdeps = [f"Tr{j}_{ts}_{dt}"]
                if ts >= 2:
                    sdeps += [f"Vffe{j}_{ts-2}_{dt}"]
                elif j >= 1:
                    sdeps += [f"Vffe{j-1}_{ts+2}_{dt}"]
                sc.add("scalar",
                       lambda e, ts=ts, dt=dt, b=b: e.activation(
                           r_sb[:, ts % 2, dt * 512:(dt + 1) * 512],
                           pb[b][:, 0:512], AF.Sigmoid),
                       deps=sdeps, event=f"Ssig{j}_{ts}_{dt}")
                bank_free[b] = f"Ssig{j}_{ts}_{dt}"
            for et in range(2):
                b = nb()

                def vgrp(e, j=j, ts=ts, et=et, b=b):
                    for dc in range(NCH):
                        ins = e.matmul(pb[b][:, 0:512],
                                       kact[:, dc, ts * 128:(ts + 1) * 128],
                                       W4[:, dc, et * 512:(et + 1) * 512],
                                       start=(dc == 0), stop=(dc == NCH - 1))
                    return ins

                deps = bank_dep(b)
                if ts == 0 and et == 0:
                    deps += [f"Vka{j}_{NCH-1}"]
                    if j == 0:
                        deps += ["GWvf"]
                sc.add("tensor", vgrp, deps=deps, event=f"Tv{j}_{ts}_{et}")

                def ffev(e, j=j, ts=ts, et=et, b=b):
                    e.tensor_mul(fftmp[:, et * 512:(et + 1) * 512],
                                 r_sb[:, ts % 2, et * 512:(et + 1) * 512],
                                 pb[b][:, 0:512])
                    e.drain()
                    return e.tensor_add(
                        part_sb[:, ts % 2, et * 512:(et + 1) * 512],
                        x_sb[:, j % 2, ts, et * 512:(et + 1) * 512],
                        fftmp[:, et * 512:(et + 1) * 512])

                vdeps = [f"Tv{j}_{ts}_{et}", f"Ssig{j}_{ts}_{et}"]
                if et == 0:
                    if ts >= 2:
                        vdeps += [f"Gpart{j}_{ts-2}"]
                    elif j >= 1:
                        vdeps += [f"Gpart{j-1}_{ts+2}"]
                sc.add("vector", ffev, deps=vdeps, event=f"Vffe{j}_{ts}_{et}")
                bank_free[b] = f"Vffe{j}_{ts}_{et}"
        sc.add("gpsimd",
               lambda e, j=j: e.dma_start(
                   out=dxnT[j],
                   in_=xnT[:, j % 2, :, XOFF:XOFF + SB]),
               deps=xnT_done(j), event=f"Gxntl{j}", dma=True)
        for half in range(2):
            sc.add("gpsimd",
                   lambda e, j=j, half=half: e.dma_start(
                       out=dqT[2 * j + half],
                       in_=qT_sb[:, :, half * IB:(half + 1) * IB]),
                   deps=[f"Sqev{j}_{NCH-1}"], event=f"GqT{j}_{half}", dma=True)
        for ts in range(4):
            sc.add("sync",
                   lambda e, j=j, ts=ts: e.dma_start(
                       out=dpart[j * SB + ts * 128:j * SB + (ts + 1) * 128, :],
                       in_=part_sb[:, ts % 2, :]),
                   deps=[f"Vffe{j}_{ts}_1"], event=f"Gpart{j}_{ts}", dma=True)


    def consts(e):
        e.memset(ones_bf[:, :], 1.0)
        e.memset(ones_f[:, :], 1.0)
        e.memset(rs4[:, :], 0.0)
        e.memset(one_f[:, :], 1.0)
        e.memset(negc[:, :], -CSHIFT)
        ins = e.memset(xnT[:, 0, :, 15:16], 0.0)
        e.drain()
        return ins

    sc.add("vector", consts, event="Vconst")
    sc.add("sync", lambda e: e.dma_start(out=ident[:, :], in_=dident),
           event="Lident", dma=True)

    def emit_wcast(wname, wd, wsb, deps=()):
        sc.add("gpsimd",
               lambda e, wd=wd, wsb=wsb: e.dma_start(
                   out=wsb[:, :, :], in_=wd.rearrange("(a p) c -> p a c", p=128)),
               deps=deps, event=wname, dma=True)

    emit_wcast("GWq", dWq, W1)

    for rnd in range(NSB + 1):
        if rnd <= NSB - 1:
            j = rnd
            deps = []
            if j >= 2:
                deps += [f"Vffe{j-2}_3_1", f"Sxnev{j-2}"]
            sc.add("sync",
                   lambda e, j=j: e.dma_start(
                       out=x_sb[:, j % 2, :, :],
                       in_=dx[j * SB:(j + 1) * SB, :].rearrange("(a p) h -> p a h", p=128)),
                   deps=deps, event=f"Lx{j}", dma=True)
            if j == 0:
                sc.add("sync", lambda e: e.dma_start(out=mur[:, :], in_=dmur),
                       event="Lmur", dma=True)
                sc.add("sync", lambda e: e.dma_start(out=muk[:, :], in_=dmuk),
                       event="Lmuk", dma=True)
        if rnd >= 1:
            F_vec_build(rnd - 1)
            F_qT(rnd - 1)
        if rnd == NSB:
            emit_wcast("GWk", dWk, W1, deps=[f"Tq{NSB-1}_{NCH-1}"])
        if rnd <= NSB - 1:
            LN_stats_v(rnd)
            LN_stats_s(rnd)
            LN_math(rnd)
        if rnd >= 1:
            F_kpre(rnd - 1,
                   (lambda r=rnd: LN_evac_scalar(r)) if rnd <= NSB - 1 else None)
        elif rnd <= NSB - 1:
            pass
        if rnd == 0:
            LN_evac(0)
        elif rnd <= NSB - 1:
            LN_evac(rnd)
        if rnd >= 1:
            F_rv(rnd - 1)
            c = rnd - 1
            sc.add("gpsimd",
                   lambda e, c=c: e.dma_start(
                       out=WvC[:, c, :], in_=dWv[c * 128:(c + 1) * 128, :]),
                   event=f"GWv_{c}", dma=True)
        if rnd == 0:
            emit_wcast("GWkf", dWkf, W3)
            emit_wcast("GWr", dWr, W2)
            emit_wcast("GWvf", dWvf, W4)


    border = [7] + list(range(7))
    for pos, k in enumerate(border):
        buf = (pos + 1) % 2
        if pos >= 1:
            deps = [f"Gxntl{k}"]
            if pos == 1:
                deps += [f"Tq{AL-1}_{NCH-1}", f"Vxk{AL-1}", f"Vxr{AL-1}",
                         f"Gxntl{AL-1}", f"Vprev{AL}"]
            elif pos == 2:
                deps += [f"Tvv{border[0]}_3_1"]
            else:
                deps += [f"Tvv{border[pos-2]}_3_1"]
            sc.add("sync",
                   lambda e, k=k, buf=buf: e.dma_start(
                       out=xnT[:, buf, :, XOFF:XOFF + SB],
                       in_=dxnT[k]),
                   deps=deps, event=f"LxnB{k}", dma=True)
        for cc in range(NCH):
            b = nb()

            def ktgrp(e, k=k, cc=cc, b=b, buf=buf):
                for hc in range(NCH):
                    ins = e.matmul(pb[b][:, 0:512],
                                   W1[:, hc, cc * 128:(cc + 1) * 128],
                                   xnT[:, buf, hc, XOFF:XOFF + SB],
                                   start=(hc == 0), stop=(hc == NCH - 1))
                return ins

            deps = bank_dep(b)
            if cc == 0:
                deps += ["GWk"]
                if pos == 0:
                    deps += xnT_done(7)
                else:
                    deps += [f"LxnB{k}"]
            sc.add("tensor", ktgrp, deps=deps, event=f"Tkt{k}_{cc}")
            sdeps = [f"Tkt{k}_{cc}"]
            if pos == 0 and cc == 0:
                sdeps += [f"Vffe{AL}_3_1", f"Gxnst{AL}"]
            sc.add("scalar",
                   lambda e, k=k, cc=cc, b=b: e.activation(
                       kT[:, cc, k * SB:(k + 1) * SB], pb[b][:, 0:512], AF.Copy),
                   deps=sdeps, event=f"Sktev{k}_{cc}")
            bank_free[b] = f"Sktev{k}_{cc}"
        for ss in range(4):
            for ct in range(2):
                b = nb()

                def vvgrp(e, k=k, ss=ss, ct=ct, b=b, buf=buf):
                    for hc in range(NCH):
                        ins = e.matmul(
                            pb[b][:, 0:512],
                            xnT[:, buf, hc, XOFF + ss * 128:XOFF + (ss + 1) * 128],
                            WvC[:, hc, ct * 512:(ct + 1) * 512],
                            start=(hc == 0), stop=(hc == NCH - 1))
                    return ins

                deps = bank_dep(b)
                if ss == 0 and ct == 0 and pos == 0:
                    deps += [f"GWv_{NCH-1}"]
                sc.add("tensor", vvgrp, deps=deps, event=f"Tvv{k}_{ss}_{ct}")
                vdeps = [f"Tvv{k}_{ss}_{ct}"]
                if pos == 0 and ss == 0 and ct == 0:
                    vdeps += [f"Gxnst{AL}", f"Ssig{AL}_3_1"]
                sc.add("vector",
                       lambda e, k=k, ss=ss, ct=ct, b=b: e.tensor_copy(
                           v_sb[:, k * 4 + ss, ct * 512:(ct + 1) * 512],
                           pb[b][:, 0:512]),
                       deps=vdeps, event=f"Vvev{k}_{ss}_{ct}")
                bank_free[b] = f"Vvev{k}_{ss}_{ct}"

    emit_wcast("GWo", dWo, W1, deps=[f"Tkt{border[-1]}_{NCH-1}"])

    BL = border[-1]

    def pv_bank(c):
        return (c + 3) % 8

    for n in range(NIB):
        if n == 0:
            for nn in (0, 1):
                sc.add("sync",
                       lambda e, nn=nn: e.dma_start(
                           out=qTd[:, nn % 2, :, :],
                           in_=dqT[nn]),
                       deps=[f"GqT0_{nn}", f"Gpart{AL}_3", f"Vffe{AL}_3_1"],
                       event=f"LqT{nn}", dma=True)
            sc.add("sync",
                   lambda e: e.dma_start(
                       out=partd[:, :, :],
                       in_=dpart[0:IB, :].rearrange("(a p) h -> p a h", p=128)),
                   deps=[f"Gpart{AL}_3", f"Vffe{AL}_3_1", f"Gpart0_1",
                         f"Tvv{border[-1]}_3_1"],
                   event="Lpart0", dma=True)
        if n + 2 < NIB:
            nn = n + 2
            sc.add("sync",
                   lambda e, nn=nn: e.dma_start(
                       out=qTd[:, nn % 2, :, :],
                       in_=dqT[nn]),
                   deps=[f"GqT{nn//2}_{nn%2}", f"Ts{n}_{NJ-1}"],
                   event=f"LqT{nn}", dma=True)
        if n + 1 < NIB:
            nn = n + 1
            sc.add("sync",
                   lambda e, nn=nn: e.dma_start(
                       out=partd[:, :, :],
                       in_=dpart[nn * IB:(nn + 1) * IB, :].rearrange(
                           "(a p) h -> p a h", p=128)),
                   deps=[f"Gpart{nn//2}_{(nn%2)*2+1}", f"Gout{n}"],
                   event=f"Lpart{nn}", dma=True)
        for jb in range(NJ):
            b = jb % 2

            def sgrp(e, n=n, jb=jb, b=b):
                for cc in range(NCH):
                    ins = e.matmul(pb[b][:, 0:IB],
                                   kT[:, cc, jb * 128:(jb + 1) * 128],
                                   qTd[:, n % 2, cc, :],
                                   start=(cc == 0), stop=(cc == NCH - 1))
                return ins

            deps = bank_dep(b)
            if jb == 0:
                deps += [f"LqT{n}"]
                if n == 0:
                    deps += [f"Sktev{BL}_{NCH-1}"]
            sc.add("tensor", sgrp, deps=deps, event=f"Ts{n}_{jb}")
            sdeps = [f"Ts{n}_{jb}"]
            if n == 0 and jb == 0:
                sdeps += ["Vconst", f"GqT{AL}_1", f"Gpart{AL}_3", f"Vffe{AL}_3_1"]
            sc.add("scalar",
                   lambda e, n=n, jb=jb, b=b: e.activation(
                       PT[:, jb, :], pb[b][:, 0:IB], AF.Exp,
                       bias=negc[:, :], scale=SCALE),
                   deps=sdeps, event=f"Sexp{n}_{jb}")
            bank_free[b] = f"Sexp{n}_{jb}"

        def pvgrp(c, b):
            def f(e, n=n, c=c, b=b):
                for jb in range(NJ):
                    if c == 0:
                        wait_ev(e, f"Sexp{n}_{jb}")
                    ins = e.matmul(pb[b][:, 0:IB],
                                   v_sb[:, jb, c * 128:(c + 1) * 128],
                                   PT[:, jb, :],
                                   start=(jb == 0), stop=(jb == NJ - 1))
                return ins
            return f

        def pv_bank(c):
            return (c + 3) % 8

        for c in range(NCH):
            deps = bank_dep(pv_bank(c))
            if c == 0 and n == 0:
                deps += [f"Vvev{BL}_3_1"]
            sc.add("tensor", pvgrp(c, pv_bank(c)), deps=deps, event=f"Tpv{n}_{c}")
        for c in range(NCH):
            vdeps = [f"Tpv{n}_{c}"]
            if n == 0 and c == 0:
                vdeps += [f"Ssumsq{AL}", f"Gpart{AL}_3"]
            sc.add("vector",
                   lambda e, c=c: e.tensor_copy(ctxT[:, c, :],
                                                pb[pv_bank(c)][:, 0:IB]),
                   deps=vdeps, event=f"Vctx{n}_{c}")
            bank_free[pv_bank(c)] = f"Vctx{n}_{c}"

        def rsgrp(e, n=n):
            for jb in range(NJ):
                t = jb % 4
                ins = e.matmul(pb[3 + t][32 * t:32 * t + 1, 0:IB], ones_bf[:, :],
                               PT[:, jb, :],
                               start=(jb < 4), stop=(jb >= NJ - 4),
                               tile_position=(0, 32 * t))
            return ins

        rdeps = bank_dep(3) + bank_dep(4) + bank_dep(5) + bank_dep(6)
        if n == 0:
            rdeps += ["Vconst"]
        sc.add("tensor", rsgrp, deps=rdeps, event=f"Trs{n}")

        def rscopy(e):
            for t in range(4):
                ins = e.activation(rs4[32 * t:32 * t + 1, :],
                                   pb[3 + t][32 * t:32 * t + 1, 0:IB], AF.Copy)
            return ins

        sc.add("scalar", rscopy, deps=[f"Trs{n}"], event=f"Srsc{n}")
        for t in range(4):
            bank_free[3 + t] = f"Srsc{n}"

        def rstgrp(e):
            e.matmul(pb[3][:, 256:257], rs4[:, 0:128], ones_f[:, :],
                     start=True, stop=False)
            return e.matmul(pb[3][:, 257:258], rs4[:, 128:256], ones_f[:, :],
                            start=False, stop=True)

        pjbank = [4, 5, 6, 7]
        g = 0
        for isub in range(2):
            for ot in range(2):
                b = pjbank[g]

                def pgrp(e, n=n, isub=isub, ot=ot, b=b):
                    for cc in range(NCH):
                        ins = e.matmul(pb[b][:, 0:512],
                                       ctxT[:, cc, isub * 128:(isub + 1) * 128],
                                       W1[:, cc, ot * 512:(ot + 1) * 512],
                                       start=(cc == 0), stop=(cc == NCH - 1))
                    return ins

                deps = [f"Vctx{n}_{NCH-1}"] + bank_dep(b)
                if n == 0 and g == 0:
                    deps += ["GWo"]
                sc.add("tensor", pgrp, deps=deps, event=f"Tpj{n}_{g}")
                if g == 0:
                    sc.add("tensor", rstgrp, deps=[f"Srsc{n}"], event=f"TrsT{n}")
                    bank_free[3] = f"Vrec{n}"
                    sc.add("vector",
                           lambda e: e.reciprocal(recip[:, :], pb[3][:, 256:258]),
                           deps=[f"TrsT{n}"], event=f"Vrec{n}", drain_after=True)
                sc.add("vector",
                       lambda e, n=n, isub=isub, ot=ot, b=b: e.scalar_tensor_tensor(
                           partd[:, isub, ot * 512:(ot + 1) * 512],
                           pb[b][:, 0:512], recip[:, isub:isub + 1],
                           partd[:, isub, ot * 512:(ot + 1) * 512],
                           ALU.mult, ALU.add),
                       deps=[f"Tpj{n}_{g}", f"Lpart{n}", f"Vrec{n}"],
                       event=f"Vstt{n}_{g}")
                bank_free[b] = f"Vstt{n}_{g}"
                g += 1
        if n == NIB - 1:
            for g in range(4):
                isub, ot = g // 2, g % 2
                sc.add("gpsimd",
                       lambda e, n=n, isub=isub, ot=ot: e.dma_start(
                           out=dout[n * IB + isub * 128:n * IB + (isub + 1) * 128,
                                    ot * 512:(ot + 1) * 512].rearrange(
                                        "(a p) h -> p a h", p=128),
                           in_=partd[:, isub:isub + 1, ot * 512:(ot + 1) * 512]),
                       deps=[f"Vstt{n}_{g}"],
                       event=f"Gout{n}" if g == 3 else None, dma=True)
        else:
            sc.add("gpsimd",
                   lambda e, n=n: e.dma_start(
                       out=dout[n * IB:(n + 1) * IB, :].rearrange("(a p) h -> p a h", p=128),
                       in_=partd[:, :, :]),
                   deps=[f"Vstt{n}_3"], event=f"Gout{n}", dma=True)

    sc.add("sync", lambda e: e.nop(), deps=[f"Gout{NIB-1}"])

    sc.resolve()

    @block.sync
    def _(e):
        sc.emit("sync", e, sems)

    @block.gpsimd
    def _(e):
        sc.emit("gpsimd", e, sems)

    @block.vector
    def _(e):
        sc.emit("vector", e, sems)

    @block.scalar
    def _(e):
        sc.emit("scalar", e, sems)

    @block.tensor
    def _(e):
        sc.emit("tensor", e, sems)

    st.close()
    return nc


def kernel(L0_attn_x, L0_attn_W_qkv, L0_attn_W_out, L0_ffn_W_r, L0_ffn_W_k,
           L0_ffn_W_v, L0_ffn_mu_r, L0_ffn_mu_k):
    x = np.asarray(L0_attn_x, np.float32)
    Wqkv = np.asarray(L0_attn_W_qkv, np.float32)
    ins = dict(
        WqT=np.ascontiguousarray(Wqkv[0::3].T),
        WkT=np.ascontiguousarray(Wqkv[1::3].T),
        WvT=np.ascontiguousarray(Wqkv[2::3].T),
        WoT=np.ascontiguousarray(np.asarray(L0_attn_W_out, np.float32).T),
        WrT=np.ascontiguousarray(np.asarray(L0_ffn_W_r, np.float32).T),
        WkfT=np.ascontiguousarray(np.asarray(L0_ffn_W_k, np.float32).T),
        WvfT=np.ascontiguousarray(np.asarray(L0_ffn_W_v, np.float32).T),
        mur=np.ascontiguousarray(np.asarray(L0_ffn_mu_r, np.float32).reshape(NCH, 128).T),
        muk=np.ascontiguousarray(np.asarray(L0_ffn_mu_k, np.float32).reshape(NCH, 128).T),
        ident=np.eye(128, dtype=np.float32),
    )
    nc = build_program()
    in_maps = [dict(x=np.ascontiguousarray(x[b]), **ins) for b in range(8)]
    res = run_bass_kernel_spmd(nc, in_maps, core_ids=list(range(8)))
    return np.stack([np.asarray(r["out"], np.float32) for r in res.results])


# version 22
# speedup vs baseline: 1.0225x; 1.0225x over previous
import contextlib

import numpy as np
import ml_dtypes

import concourse.bass as bass
import concourse.mybir as mybir
from concourse.bass_utils import run_bass_kernel_spmd

F32 = mybir.dt.float32
BF16 = mybir.dt.bfloat16
AF = mybir.ActivationFunctionType
ALU = mybir.AluOpType
AX = mybir.AxisListType

S = 4096
H = 1024
NCH = 8
SB = 512
NSB = S // SB
IB = 256
NIB = S // IB
NJ = S // 128
CSHIFT = 20.0
SCALE = 1.0 / 32.0
XST = 528
XOFF = 16

ENGINES = ("sync", "gpsimd", "vector", "scalar", "tensor")
SEMS = ENGINES + ("scalardma",)


class Op:
    __slots__ = ("engine", "fn", "deps", "event", "dma", "drain_after", "idx", "sem")

    def __init__(self, engine, fn, deps, event, dma, drain_after):
        self.engine = engine
        self.fn = fn
        self.deps = list(deps)
        self.event = event
        self.dma = dma
        self.drain_after = drain_after


class Sched:

    def __init__(self):
        self.ops = []
        self.ev_engine = {}

    def add(self, engine, fn, deps=(), event=None, dma=False, drain_after=False):
        op = Op(engine, fn, deps, event, dma, drain_after)
        op.sem = "scalardma" if (dma and engine == "scalar") else engine
        op.idx = len(self.ops)
        self.ops.append(op)
        if event is not None:
            assert event not in self.ev_engine, f"dup event {event}"
            self.ev_engine[event] = (op.sem, op.idx)
        return op

    def resolve(self):
        referenced = set()
        for op in self.ops:
            for d in op.deps:
                assert d in self.ev_engine, f"unknown dep {d}"
                referenced.add(d)
        counts = {e: 0 for e in SEMS}
        self.ev_val = {}
        self.inc_amt = {}
        for op in self.ops:
            if op.dma:
                amt = 16
            elif op.event is not None and op.event in referenced:
                amt = 1
            else:
                amt = 0
            counts[op.sem] += amt
            self.inc_amt[op.idx] = amt
            if op.event is not None:
                self.ev_val[op.event] = (op.sem, counts[op.sem])
        for op in self.ops:
            for d in op.deps:
                de, di = self.ev_engine[d]
                if de == op.engine:
                    assert di < op.idx, f"same-engine forward dep {d}"

    def emit(self, engine, eng, sems):
        last_wait = {}
        for op in self.ops:
            if op.engine != engine:
                continue
            for d in op.deps:
                dsem, _ = self.ev_engine[d]
                if dsem == engine:
                    continue
                val = self.ev_val[d][1]
                if last_wait.get(dsem, -1) >= val:
                    continue
                eng.wait_ge(sems[dsem], val)
                last_wait[dsem] = val
            instr = op.fn(eng)
            amt = self.inc_amt[op.idx]
            if amt:
                assert instr is not None, f"op {op.event} returned no instr"
                instr.then_inc(sems[op.sem], amt)
            if op.drain_after:
                eng.drain()


def build_program(debug=False):
    nc = bass.Bass("TRN2", target_bir_lowering=False, detect_race_conditions=False)

    dx = nc.dram_tensor("x", [S, H], F32, kind="ExternalInput").ap()
    dWq = nc.dram_tensor("WqT", [H, H], F32, kind="ExternalInput").ap()
    dWk = nc.dram_tensor("WkT", [H, H], F32, kind="ExternalInput").ap()
    dWv = nc.dram_tensor("WvT", [H, H], F32, kind="ExternalInput").ap()
    dWo = nc.dram_tensor("WoT", [H, H], F32, kind="ExternalInput").ap()
    dWr = nc.dram_tensor("WrT", [H, H], F32, kind="ExternalInput").ap()
    dWkf = nc.dram_tensor("WkfT", [H, H], F32, kind="ExternalInput").ap()
    dWvf = nc.dram_tensor("WvfT", [H, H], F32, kind="ExternalInput").ap()
    dmur = nc.dram_tensor("mur", [128, NCH], F32, kind="ExternalInput").ap()
    dmuk = nc.dram_tensor("muk", [128, NCH], F32, kind="ExternalInput").ap()
    dident = nc.dram_tensor("ident", [128, 128], F32, kind="ExternalInput").ap()
    dout = nc.dram_tensor("out", [S, H], F32, kind="ExternalOutput").ap()
    skind = "ExternalOutput" if debug else "Internal"
    dxn = nc.dram_tensor("xn_nat", [S, H], BF16, kind=skind).ap()
    dxnT = nc.dram_tensor("xnT_lin", [NSB, 128, NCH, SB], BF16, kind=skind).ap()
    dqT = nc.dram_tensor("qT_lin", [NIB, 128, NCH, IB], BF16, kind=skind).ap()
    dpart = nc.dram_tensor("part", [S, H], F32, kind=skind).ap()

    cursor = [16512]

    def alloc(name, shape, dtype, at=None):
        if at is not None:
            cursor[0] = at
        off = (cursor[0] + 31) // 32 * 32
        nbytes = int(np.prod(shape[1:])) * mybir.dt.size(dtype)
        t = nc.alloc_sbuf_tensor_at(name, list(shape), dtype, offset=off)
        cursor[0] = off + nbytes
        return t

    lnsum = alloc("lnsum", [128, 4], F32)
    lnsumsq = alloc("lnsumsq", [128, 4], F32)
    lnm = alloc("lnm", [128, 4], F32)
    lnex2 = alloc("lnex2", [128, 4], F32)
    lnt0 = alloc("lnt0", [128, 4], F32)
    lnt1 = alloc("lnt1", [128, 4], F32)
    lnt2 = alloc("lnt2", [128, 4], F32)
    lnr0 = alloc("lnr0", [128, 4], F32)
    lnrstd = alloc("lnrstd", [128, 4], F32)
    lnbias = alloc("lnbias", [128, 4], F32)
    mur = alloc("mur_sb", [128, NCH], F32)
    muk = alloc("muk_sb", [128, NCH], F32)
    ones_bf = alloc("ones_bf", [128, 1], BF16)
    ones_f = alloc("ones_f", [128, 1], F32)
    one_f = alloc("one_f", [1, 8], F32)
    negc = alloc("negc", [128, 1], F32)
    rs_f = alloc("rs_f", [1, IB], F32)
    recip = alloc("recip", [128, 2], F32)
    ident = alloc("ident_sb", [128, 128], F32)
    xnT = alloc("xnT", [128, 2, NCH, XST], BF16)
    W1 = alloc("W1", [128, NCH, H], BF16)
    FIXED_END = cursor[0]

    W2 = alloc("W2", [128, NCH, H], BF16)
    W3 = alloc("W3", [128, NCH, H], BF16)
    W4 = alloc("W4", [128, NCH, H], BF16)
    x_sb = alloc("x_sb", [128, 2, 4, H], F32)
    xn_sb = alloc("xn_sb", [128, 4, H], BF16)
    junk = alloc("junk", [128, H], F32)
    relu_t = alloc("relu_t", [128, 2, 512], F32)
    d_sb = alloc("d_sb", [128, NCH, SB], BF16)
    xr_sb = alloc("xr_sb", [128, NCH, SB], BF16)
    xk_sb = alloc("xk_sb", [128, NCH, SB], BF16)
    kact = alloc("kact", [128, NCH, SB], BF16)
    r_sb = alloc("r_sb", [128, 2, H], F32)
    fftmp = alloc("fftmp", [128, H], F32)
    qT_sb = alloc("qT_sb", [128, NCH, SB], BF16)
    part_sb = alloc("part_sb", [128, 2, H], F32)
    A_END = cursor[0]
    WvC = alloc("WvC", [128, NCH, H], BF16)
    rs4 = alloc("rs4", [128, IB], F32)
    assert cursor[0] <= 229376, cursor[0]

    kT = alloc("kT", [128, NCH, S], BF16, at=FIXED_END)
    v_sb = alloc("v_sb", [128, NJ, H], BF16)
    BC_BASE = cursor[0]
    PT = alloc("PT", [128, NJ, IB], BF16, at=BC_BASE)
    qTd = alloc("qTd", [128, 2, NCH, IB], BF16)
    partd = alloc("partd", [128, 2, H], F32)
    ctxT = alloc("ctxT", [128, NCH, IB], BF16)
    D_END = cursor[0]
    assert D_END <= 229376, D_END

    st = contextlib.ExitStack()
    pb = [st.enter_context(nc.psum_tensor(f"pb{i}", [128, 512], F32)) for i in range(8)]
    sems = {e: st.enter_context(nc.semaphore(f"sem_{e}")) for e in SEMS}
    block = st.enter_context(nc.Block())

    sc = Sched()
    bank_free = {i: None for i in range(8)}

    def wait_ev(e, ev):
        s, v = sc.ev_val[ev]
        e.wait_ge(sems[s], v)
    _abank = [0]

    def nb():
        b = _abank[0] % 8
        _abank[0] += 1
        return b

    def bank_dep(i):
        e = bank_free[i]
        return [e] if e else []

    AL = NSB - 1

    def xnT_done(j):
        if j == 0:
            return ["VxnT0_7"]
        return [f"LxnT{j}_{NCH-1}"]

    def F_vec_build(j):
        if j >= 1:
            sc.add("vector",
                   lambda e, j=j: e.tensor_copy(
                       xnT[:, j % 2, :, 15:16], xnT[:, (j - 1) % 2, :, XST - 1:XST]),
                   deps=xnT_done(j) + xnT_done(j - 1),
                   event=f"Vprev{j}", drain_after=True)
            dprev = [f"Vprev{j}"]
        else:
            dprev = xnT_done(j) + ["Lmur", "Lmuk"]
        sc.add("vector",
               lambda e, j=j: e.tensor_sub(
                   d_sb[:, :, :], xnT[:, j % 2, :, XOFF:XOFF + SB],
                   xnT[:, j % 2, :, XOFF - 1:XOFF - 1 + SB]),
               deps=dprev, drain_after=True)
        for cc in range(NCH):
            sc.add("vector",
                   lambda e, j=j, cc=cc: e.scalar_tensor_tensor(
                       xr_sb[:, cc, :], d_sb[:, cc, :], mur[:, cc:cc + 1],
                       xnT[:, j % 2, cc, XOFF - 1:XOFF - 1 + SB], ALU.mult, ALU.add),
                   event=f"Vxr{j}" if cc == NCH - 1 else None)
        for cc in range(NCH):
            sc.add("vector",
                   lambda e, j=j, cc=cc: e.scalar_tensor_tensor(
                       xk_sb[:, cc, :], d_sb[:, cc, :], muk[:, cc:cc + 1],
                       xnT[:, j % 2, cc, XOFF - 1:XOFF - 1 + SB], ALU.mult, ALU.add),
                   event=f"Vxk{j}" if cc == NCH - 1 else None)

    def F_qT(j):
        for cc in range(NCH):
            b = nb()

            def qgrp(e, j=j, cc=cc, b=b):
                for hc in range(NCH):
                    ins = e.matmul(pb[b][:, 0:512],
                                   W1[:, hc, cc * 128:(cc + 1) * 128],
                                   xnT[:, j % 2, hc, XOFF:XOFF + SB],
                                   start=(hc == 0), stop=(hc == NCH - 1))
                return ins

            deps = bank_dep(b)
            if cc == 0:
                deps += xnT_done(j)
                if j == 0:
                    deps += ["GWq"]
            sc.add("tensor", qgrp, deps=deps, event=f"Tq{j}_{cc}")
            sdeps = [f"Tq{j}_{cc}"]
            if cc == 0 and j >= 1:
                sdeps += [f"GqT{j-1}_1"]
            sc.add("scalar",
                   lambda e, cc=cc, b=b: e.activation(
                       qT_sb[:, cc, :], pb[b][:, 0:512], AF.Copy),
                   deps=sdeps, event=f"Sqev{j}_{cc}")
            bank_free[b] = f"Sqev{j}_{cc}"

    def LN_stats_s(j):
        for sub in range(4):
            deps = [f"Lx{j}"]
            if sub == 0 and j >= 1:
                deps += [f"VmathA{j-1}"]
            if sub == 0 and j == 1:
                deps += ["TxnT0_7"]
            sc.add("scalar",
                   lambda e, j=j, sub=sub: e.activation(
                       junk[:, :], x_sb[:, j % 2, sub, :], AF.Square,
                       accum_out=lnsumsq[:, sub:sub + 1]),
                   deps=deps,
                   event=f"Ssumsq{j}" if sub == 3 else None)

    def LN_stats_v(j):
        for sub in range(4):
            sc.add("vector",
                   lambda e, j=j, sub=sub: e.reduce_sum(
                       lnsum[:, sub:sub + 1], x_sb[:, j % 2, sub, :], axis=AX.X),
                   deps=[f"Lx{j}"], drain_after=(sub == 3))

    def LN_math(j):
        def mathA(e):
            e.tensor_scalar_mul(lnm[:, :], lnsum[:, :], 1.0 / H)
            e.tensor_scalar_mul(lnex2[:, :], lnsumsq[:, :], 1.0 / H)
            e.drain()
            e.tensor_mul(lnt0[:, :], lnm[:, :], lnm[:, :])
            e.drain()
            e.tensor_sub(lnt1[:, :], lnex2[:, :], lnt0[:, :])
            e.drain()
            e.tensor_scalar_add(lnt1[:, :], lnt1[:, :], 1e-5)
            e.drain()
            return e.reciprocal(lnt0[:, :], lnt1[:, :])

        sc.add("vector", mathA, deps=[f"Ssumsq{j}"], event=f"VmathA{j}")
        sc.add("scalar", lambda e: e.activation(lnr0[:, :], lnt0[:, :], AF.Sqrt),
               deps=[f"VmathA{j}"], event=f"Ssqrt{j}")

        def mathB(e):
            e.tensor_mul(lnt2[:, :], lnr0[:, :], lnr0[:, :])
            e.drain()
            e.tensor_mul(lnt2[:, :], lnt2[:, :], lnt1[:, :])
            e.drain()
            e.tensor_scalar(lnt2[:, :], lnt2[:, :], -0.5, 1.5, ALU.mult, ALU.add)
            e.drain()
            e.tensor_mul(lnrstd[:, :], lnr0[:, :], lnt2[:, :])
            e.drain()
            return e.scalar_tensor_tensor(
                lnbias[:, :], lnm[:, :], -1.0, lnrstd[:, :], ALU.mult, ALU.mult)

        sc.add("vector", mathB,
               deps=[f"Ssqrt{j}"] + ([f"Sxnev{j-1}"] if j >= 1 else []),
               event=f"VmathB{j}")

    def F_kpre(j, evac_fn=None):
        for dc in range(NCH):
            if dc == 4 and evac_fn is not None:
                evac_fn()
            b = nb()

            def kgrp(e, j=j, dc=dc, b=b):
                for cc in range(NCH):
                    ins = e.matmul(pb[b][:, 0:512],
                                   W3[:, cc, dc * 128:(dc + 1) * 128],
                                   xk_sb[:, cc, :],
                                   start=(cc == 0), stop=(cc == NCH - 1))
                return ins

            deps = bank_dep(b)
            if dc == 0:
                deps += [f"Vxk{j}"]
                if j == 0:
                    deps += ["GWkf"]
            sc.add("tensor", kgrp, deps=deps, event=f"Tkp{j}_{dc}")
            sdeps = [f"Tkp{j}_{dc}"]
            if dc >= 2:
                sdeps += [f"Vka{j}_{dc-2}"]
            elif j >= 1:
                sdeps += [f"Vka{j-1}_{dc+6}"]
            sc.add("scalar",
                   lambda e, dc=dc, b=b: e.activation(
                       relu_t[:, dc % 2, :], pb[b][:, 0:512], AF.Relu),
                   deps=sdeps, event=f"Srelu{j}_{dc}")
            bank_free[b] = f"Srelu{j}_{dc}"
            sc.add("vector",
                   lambda e, dc=dc: e.tensor_mul(
                       kact[:, dc, :], relu_t[:, dc % 2, :], relu_t[:, dc % 2, :]),
                   deps=[f"Srelu{j}_{dc}"], event=f"Vka{j}_{dc}")

    def LN_evac_scalar(j):
        for sub in range(4):
            deps = [f"VmathB{j}"]
            if sub == 0 and j >= 2:
                deps += [f"Gxnst{j-1}"]
            if j == 0:
                dst = junk if sub % 2 == 0 else fftmp
                if sub >= 2:
                    deps += [f"TxnT0_{(sub - 2) * 2 + 1}"]
                sc.add("scalar",
                       lambda e, sub=sub, dst=dst: e.activation(
                           dst[:, :], x_sb[:, 0, sub, :], AF.Identity,
                           bias=lnbias[:, sub:sub + 1], scale=lnrstd[:, sub:sub + 1]),
                       deps=deps,
                       event="Sxnev0" if sub == 3 else f"Sxnev0s{sub}")
            else:
                sc.add("scalar",
                       lambda e, j=j, sub=sub: e.activation(
                           xn_sb[:, sub, :], x_sb[:, j % 2, sub, :], AF.Identity,
                           bias=lnbias[:, sub:sub + 1], scale=lnrstd[:, sub:sub + 1]),
                       deps=deps, event=f"Sxnev{j}" if sub == 3 else None)

    def LN_evac(j):
        if j == 0:
            LN_evac_scalar(0)
            for q in range(8):
                sub, half = q // 2, q % 2
                src_t = junk if sub % 2 == 0 else fftmp
                b = 6 + (q % 2)

                def tgrp(e, sub=sub, half=half, src_t=src_t, b=b):
                    for k in range(4):
                        cc = half * 4 + k
                        ins = e.matmul(pb[b][:, k * 128:(k + 1) * 128],
                                       src_t[:, cc * 128:(cc + 1) * 128],
                                       ident[:, :],
                                       start=(k == 0), stop=(k == 3),
                                       is_transpose=True)
                    return ins

                tdeps = [f"Sxnev0s{sub}" if sub < 3 else "Sxnev0"]
                if q == 0:
                    tdeps += ["Lident"]
                if q >= 2:
                    tdeps += [f"VxnT0_{q-2}"]
                sc.add("tensor", tgrp, deps=tdeps, event=f"TxnT0_{q}")

                def tevac(e, sub=sub, half=half, b=b):
                    for k in range(4):
                        cc = half * 4 + k
                        ins = e.tensor_copy(
                            xnT[:, 0, cc, XOFF + sub * 128:XOFF + (sub + 1) * 128],
                            pb[b][:, k * 128:(k + 1) * 128])
                    return ins

                sc.add("vector", tevac, deps=[f"TxnT0_{q}"], event=f"VxnT0_{q}")
            bank_free[6] = "VxnT0_6"
            bank_free[7] = "VxnT0_7"
            return
        sc.add("gpsimd",
               lambda e, j=j: e.dma_start(
                   out=dxn[j * SB:(j + 1) * SB, :].rearrange("(a p) h -> p a h", p=128),
                   in_=xn_sb[:, :, :]),
               deps=[f"Sxnev{j}"], event=f"Gxnst{j}", dma=True)
        for cc in range(NCH):
            deps = [f"Gxnst{j}"]
            if j >= 2 and cc == 0:
                deps += [f"Vxk{j-2}", f"Tq{j-2}_{NCH-1}", f"Gxntl{j-2}"]
                if j - 1 >= 1:
                    deps += [f"Vprev{j-1}"]
            sc.add("sync",
                   lambda e, j=j, cc=cc: e.dma_start_transpose(
                       xnT[:, j % 2, cc, XOFF:XOFF + SB],
                       dxn[j * SB:(j + 1) * SB, cc * 128:(cc + 1) * 128]),
                   deps=deps, event=f"LxnT{j}_{cc}", dma=True)

    def F_rv(j):
        for ts in range(4):
            for dt in range(2):
                b = nb()

                def rgrp(e, j=j, ts=ts, dt=dt, b=b):
                    for cc in range(NCH):
                        ins = e.matmul(pb[b][:, 0:512],
                                       xr_sb[:, cc, ts * 128:(ts + 1) * 128],
                                       W2[:, cc, dt * 512:(dt + 1) * 512],
                                       start=(cc == 0), stop=(cc == NCH - 1))
                    return ins

                deps = bank_dep(b)
                if ts == 0 and dt == 0:
                    deps += [f"Vxr{j}"]
                    if j == 0:
                        deps += ["GWr"]
                sc.add("tensor", rgrp, deps=deps, event=f"Tr{j}_{ts}_{dt}")
                sdeps = [f"Tr{j}_{ts}_{dt}"]
                if ts >= 2:
                    sdeps += [f"Vffe{j}_{ts-2}_{dt}"]
                elif j >= 1:
                    sdeps += [f"Vffe{j-1}_{ts+2}_{dt}"]
                sc.add("scalar",
                       lambda e, ts=ts, dt=dt, b=b: e.activation(
                           r_sb[:, ts % 2, dt * 512:(dt + 1) * 512],
                           pb[b][:, 0:512], AF.Sigmoid),
                       deps=sdeps, event=f"Ssig{j}_{ts}_{dt}")
                bank_free[b] = f"Ssig{j}_{ts}_{dt}"
            for et in range(2):
                b = nb()

                def vgrp(e, j=j, ts=ts, et=et, b=b):
                    for dc in range(NCH):
                        ins = e.matmul(pb[b][:, 0:512],
                                       kact[:, dc, ts * 128:(ts + 1) * 128],
                                       W4[:, dc, et * 512:(et + 1) * 512],
                                       start=(dc == 0), stop=(dc == NCH - 1))
                    return ins

                deps = bank_dep(b)
                if ts == 0 and et == 0:
                    deps += [f"Vka{j}_{NCH-1}"]
                    if j == 0:
                        deps += ["GWvf"]
                sc.add("tensor", vgrp, deps=deps, event=f"Tv{j}_{ts}_{et}")

                def ffev(e, j=j, ts=ts, et=et, b=b):
                    e.tensor_mul(fftmp[:, et * 512:(et + 1) * 512],
                                 r_sb[:, ts % 2, et * 512:(et + 1) * 512],
                                 pb[b][:, 0:512])
                    e.drain()
                    return e.tensor_add(
                        part_sb[:, ts % 2, et * 512:(et + 1) * 512],
                        x_sb[:, j % 2, ts, et * 512:(et + 1) * 512],
                        fftmp[:, et * 512:(et + 1) * 512])

                vdeps = [f"Tv{j}_{ts}_{et}", f"Ssig{j}_{ts}_{et}"]
                if et == 0:
                    if ts >= 2:
                        vdeps += [f"Gpart{j}_{ts-2}"]
                    elif j >= 1:
                        vdeps += [f"Gpart{j-1}_{ts+2}"]
                sc.add("vector", ffev, deps=vdeps, event=f"Vffe{j}_{ts}_{et}")
                bank_free[b] = f"Vffe{j}_{ts}_{et}"
        sc.add("gpsimd",
               lambda e, j=j: e.dma_start(
                   out=dxnT[j],
                   in_=xnT[:, j % 2, :, XOFF:XOFF + SB]),
               deps=xnT_done(j), event=f"Gxntl{j}", dma=True)
        for half in range(2):
            sc.add("gpsimd",
                   lambda e, j=j, half=half: e.dma_start(
                       out=dqT[2 * j + half],
                       in_=qT_sb[:, :, half * IB:(half + 1) * IB]),
                   deps=[f"Sqev{j}_{NCH-1}"], event=f"GqT{j}_{half}", dma=True)
        for ts in range(4):
            sc.add("sync",
                   lambda e, j=j, ts=ts: e.dma_start(
                       out=dpart[j * SB + ts * 128:j * SB + (ts + 1) * 128, :],
                       in_=part_sb[:, ts % 2, :]),
                   deps=[f"Vffe{j}_{ts}_1"], event=f"Gpart{j}_{ts}", dma=True)


    def consts(e):
        e.memset(ones_bf[:, :], 1.0)
        e.memset(ones_f[:, :], 1.0)
        e.memset(rs4[:, :], 0.0)
        e.memset(one_f[:, :], 1.0)
        e.memset(negc[:, :], -CSHIFT)
        ins = e.memset(xnT[:, 0, :, 15:16], 0.0)
        e.drain()
        return ins

    sc.add("vector", consts, event="Vconst")
    sc.add("sync", lambda e: e.dma_start(out=ident[:, :], in_=dident),
           event="Lident", dma=True)

    def emit_wcast(wname, wd, wsb, deps=()):
        sc.add("gpsimd",
               lambda e, wd=wd, wsb=wsb: e.dma_start(
                   out=wsb[:, :, :], in_=wd.rearrange("(a p) c -> p a c", p=128)),
               deps=deps, event=wname, dma=True)

    emit_wcast("GWq", dWq, W1)

    for rnd in range(NSB + 1):
        if rnd <= NSB - 1:
            j = rnd
            deps = []
            if j >= 2:
                deps += [f"Vffe{j-2}_3_1", f"Sxnev{j-2}"]
            sc.add("sync",
                   lambda e, j=j: e.dma_start(
                       out=x_sb[:, j % 2, :, :],
                       in_=dx[j * SB:(j + 1) * SB, :].rearrange("(a p) h -> p a h", p=128)),
                   deps=deps, event=f"Lx{j}", dma=True)
            if j == 0:
                sc.add("sync", lambda e: e.dma_start(out=mur[:, :], in_=dmur),
                       event="Lmur", dma=True)
                sc.add("sync", lambda e: e.dma_start(out=muk[:, :], in_=dmuk),
                       event="Lmuk", dma=True)
        if rnd >= 1:
            F_vec_build(rnd - 1)
            F_qT(rnd - 1)
        if rnd == NSB:
            emit_wcast("GWk", dWk, W1, deps=[f"Tq{NSB-1}_{NCH-1}"])
        if rnd <= NSB - 1:
            LN_stats_v(rnd)
            LN_stats_s(rnd)
            LN_math(rnd)
        if rnd >= 1:
            F_kpre(rnd - 1,
                   (lambda r=rnd: LN_evac_scalar(r)) if rnd <= NSB - 1 else None)
        elif rnd <= NSB - 1:
            pass
        if rnd == 0:
            LN_evac(0)
        elif rnd <= NSB - 1:
            LN_evac(rnd)
        if rnd >= 1:
            F_rv(rnd - 1)
            c = rnd - 1
            sc.add("gpsimd",
                   lambda e, c=c: e.dma_start(
                       out=WvC[:, c, :], in_=dWv[c * 128:(c + 1) * 128, :]),
                   event=f"GWv_{c}", dma=True)
        if rnd == 0:
            emit_wcast("GWkf", dWkf, W3)
            emit_wcast("GWr", dWr, W2)
            emit_wcast("GWvf", dWvf, W4)


    border = [7] + list(range(7))
    for pos, k in enumerate(border):
        buf = (pos + 1) % 2
        if pos >= 1:
            deps = [f"Gxntl{k}"]
            if pos == 1:
                deps += [f"Tq{AL-1}_{NCH-1}", f"Vxk{AL-1}", f"Vxr{AL-1}",
                         f"Gxntl{AL-1}", f"Vprev{AL}"]
            elif pos == 2:
                deps += [f"Tvv{border[0]}_3_1"]
            else:
                deps += [f"Tvv{border[pos-2]}_3_1"]
            sc.add("sync",
                   lambda e, k=k, buf=buf: e.dma_start(
                       out=xnT[:, buf, :, XOFF:XOFF + SB],
                       in_=dxnT[k]),
                   deps=deps, event=f"LxnB{k}", dma=True)
        for cc in range(NCH):
            b = nb()

            def ktgrp(e, k=k, cc=cc, b=b, buf=buf):
                for hc in range(NCH):
                    ins = e.matmul(pb[b][:, 0:512],
                                   W1[:, hc, cc * 128:(cc + 1) * 128],
                                   xnT[:, buf, hc, XOFF:XOFF + SB],
                                   start=(hc == 0), stop=(hc == NCH - 1))
                return ins

            deps = bank_dep(b)
            if cc == 0:
                deps += ["GWk"]
                if pos == 0:
                    deps += xnT_done(7)
                else:
                    deps += [f"LxnB{k}"]
            sc.add("tensor", ktgrp, deps=deps, event=f"Tkt{k}_{cc}")
            sdeps = [f"Tkt{k}_{cc}"]
            if pos == 0 and cc == 0:
                sdeps += [f"Vffe{AL}_3_1", f"Gxnst{AL}"]
            sc.add("scalar",
                   lambda e, k=k, cc=cc, b=b: e.activation(
                       kT[:, cc, k * SB:(k + 1) * SB], pb[b][:, 0:512], AF.Copy),
                   deps=sdeps, event=f"Sktev{k}_{cc}")
            bank_free[b] = f"Sktev{k}_{cc}"
        for ss in range(4):
            for ct in range(2):
                b = nb()

                def vvgrp(e, k=k, ss=ss, ct=ct, b=b, buf=buf):
                    for hc in range(NCH):
                        ins = e.matmul(
                            pb[b][:, 0:512],
                            xnT[:, buf, hc, XOFF + ss * 128:XOFF + (ss + 1) * 128],
                            WvC[:, hc, ct * 512:(ct + 1) * 512],
                            start=(hc == 0), stop=(hc == NCH - 1))
                    return ins

                deps = bank_dep(b)
                if ss == 0 and ct == 0 and pos == 0:
                    deps += [f"GWv_{NCH-1}"]
                sc.add("tensor", vvgrp, deps=deps, event=f"Tvv{k}_{ss}_{ct}")
                vdeps = [f"Tvv{k}_{ss}_{ct}"]
                if pos == 0 and ss == 0 and ct == 0:
                    vdeps += [f"Gxnst{AL}", f"Ssig{AL}_3_1"]
                sc.add("vector",
                       lambda e, k=k, ss=ss, ct=ct, b=b: e.tensor_copy(
                           v_sb[:, k * 4 + ss, ct * 512:(ct + 1) * 512],
                           pb[b][:, 0:512]),
                       deps=vdeps, event=f"Vvev{k}_{ss}_{ct}")
                bank_free[b] = f"Vvev{k}_{ss}_{ct}"

    emit_wcast("GWo", dWo, W1, deps=[f"Tkt{border[-1]}_{NCH-1}"])

    BL = border[-1]

    def pv_bank(c):
        return (c + 3) % 8

    for n in range(NIB):
        if n == 0:
            for nn in (0, 1):
                sc.add("sync",
                       lambda e, nn=nn: e.dma_start(
                           out=qTd[:, nn % 2, :, :],
                           in_=dqT[nn]),
                       deps=[f"GqT0_{nn}", f"Gpart{AL}_3", f"Vffe{AL}_3_1"],
                       event=f"LqT{nn}", dma=True)
            sc.add("sync",
                   lambda e: e.dma_start(
                       out=partd[:, :, :],
                       in_=dpart[0:IB, :].rearrange("(a p) h -> p a h", p=128)),
                   deps=[f"Gpart{AL}_3", f"Vffe{AL}_3_1", f"Gpart0_1",
                         f"Tvv{border[-1]}_3_1"],
                   event="Lpart0", dma=True)
        if n + 2 < NIB:
            nn = n + 2
            sc.add("sync",
                   lambda e, nn=nn: e.dma_start(
                       out=qTd[:, nn % 2, :, :],
                       in_=dqT[nn]),
                   deps=[f"GqT{nn//2}_{nn%2}", f"Ts{n}_{NJ-1}"],
                   event=f"LqT{nn}", dma=True)
        if n + 1 < NIB:
            nn = n + 1
            sc.add("sync",
                   lambda e, nn=nn: e.dma_start(
                       out=partd[:, :, :],
                       in_=dpart[nn * IB:(nn + 1) * IB, :].rearrange(
                           "(a p) h -> p a h", p=128)),
                   deps=[f"Gpart{nn//2}_{(nn%2)*2+1}", f"Gout{n}"],
                   event=f"Lpart{nn}", dma=True)
        for jb in range(NJ):
            b = jb % 2

            def sgrp(e, n=n, jb=jb, b=b):
                for cc in range(NCH):
                    ins = e.matmul(pb[b][:, 0:IB],
                                   kT[:, cc, jb * 128:(jb + 1) * 128],
                                   qTd[:, n % 2, cc, :],
                                   start=(cc == 0), stop=(cc == NCH - 1))
                return ins

            deps = bank_dep(b)
            if jb == 0:
                deps += [f"LqT{n}"]
                if n == 0:
                    deps += [f"Sktev{BL}_{NCH-1}"]
            sc.add("tensor", sgrp, deps=deps, event=f"Ts{n}_{jb}")
            sdeps = [f"Ts{n}_{jb}"]
            if n == 0 and jb == 0:
                sdeps += ["Vconst", f"GqT{AL}_1", f"Gpart{AL}_3", f"Vffe{AL}_3_1"]
            sc.add("scalar",
                   lambda e, n=n, jb=jb, b=b: e.activation(
                       PT[:, jb, :], pb[b][:, 0:IB], AF.Exp,
                       bias=negc[:, :], scale=SCALE),
                   deps=sdeps, event=f"Sexp{n}_{jb}")
            bank_free[b] = f"Sexp{n}_{jb}"

        def pvgrp(c, b):
            def f(e, n=n, c=c, b=b):
                for jb in range(NJ):
                    if c == 0:
                        wait_ev(e, f"Sexp{n}_{jb}")
                    ins = e.matmul(pb[b][:, 0:IB],
                                   v_sb[:, jb, c * 128:(c + 1) * 128],
                                   PT[:, jb, :],
                                   start=(jb == 0), stop=(jb == NJ - 1))
                return ins
            return f

        def pv_bank(c):
            return (c + 3) % 8

        for c in range(NCH):
            deps = bank_dep(pv_bank(c))
            if c == 0 and n == 0:
                deps += [f"Vvev{BL}_3_1"]
            sc.add("tensor", pvgrp(c, pv_bank(c)), deps=deps, event=f"Tpv{n}_{c}")
        for c in range(NCH):
            vdeps = [f"Tpv{n}_{c}"]
            if n == 0 and c == 0:
                vdeps += [f"Ssumsq{AL}", f"Gpart{AL}_3"]
            sc.add("vector",
                   lambda e, c=c: e.tensor_copy(ctxT[:, c, :],
                                                pb[pv_bank(c)][:, 0:IB]),
                   deps=vdeps, event=f"Vctx{n}_{c}")
            bank_free[pv_bank(c)] = f"Vctx{n}_{c}"

        def rsgrp(e, n=n):
            for jb in range(NJ):
                t = jb % 4
                ins = e.matmul(pb[3 + t][32 * t:32 * t + 1, 0:IB], ones_bf[:, :],
                               PT[:, jb, :],
                               start=(jb < 4), stop=(jb >= NJ - 4),
                               tile_position=(0, 32 * t))
            return ins

        rdeps = bank_dep(3) + bank_dep(4) + bank_dep(5) + bank_dep(6)
        if n == 0:
            rdeps += ["Vconst"]
        sc.add("tensor", rsgrp, deps=rdeps, event=f"Trs{n}")

        def rscopy(e):
            for t in range(4):
                ins = e.activation(rs4[32 * t:32 * t + 1, :],
                                   pb[3 + t][32 * t:32 * t + 1, 0:IB], AF.Copy)
            return ins

        sc.add("scalar", rscopy, deps=[f"Trs{n}"], event=f"Srsc{n}")
        for t in range(1, 4):
            bank_free[3 + t] = f"Srsc{n}"

        def rstgrp(e):
            e.matmul(pb[3][:, 256:257], rs4[:, 0:128], ones_f[:, :],
                     start=True, stop=False)
            return e.matmul(pb[3][:, 257:258], rs4[:, 128:256], ones_f[:, :],
                            start=False, stop=True)

        pjbank = [7, 0, 1, 2]
        g = 0
        for isub in range(2):
            for ot in range(2):
                b = pjbank[g]

                def pgrp(e, n=n, isub=isub, ot=ot, b=b):
                    for cc in range(NCH):
                        ins = e.matmul(pb[b][:, 0:512],
                                       ctxT[:, cc, isub * 128:(isub + 1) * 128],
                                       W1[:, cc, ot * 512:(ot + 1) * 512],
                                       start=(cc == 0), stop=(cc == NCH - 1))
                    return ins

                deps = [f"Vctx{n}_{NCH-1}"] + bank_dep(b)
                if n == 0 and g == 0:
                    deps += ["GWo"]
                sc.add("tensor", pgrp, deps=deps, event=f"Tpj{n}_{g}")
                if g == 0:
                    sc.add("tensor", rstgrp, deps=[f"Srsc{n}"], event=f"TrsT{n}")
                    bank_free[3] = f"Vrec{n}"
                    sc.add("vector",
                           lambda e: e.reciprocal(recip[:, :], pb[3][:, 256:258]),
                           deps=[f"TrsT{n}"], event=f"Vrec{n}", drain_after=True)
                sc.add("vector",
                       lambda e, n=n, isub=isub, ot=ot, b=b: e.scalar_tensor_tensor(
                           partd[:, isub, ot * 512:(ot + 1) * 512],
                           pb[b][:, 0:512], recip[:, isub:isub + 1],
                           partd[:, isub, ot * 512:(ot + 1) * 512],
                           ALU.mult, ALU.add),
                       deps=[f"Tpj{n}_{g}", f"Lpart{n}", f"Vrec{n}"],
                       event=f"Vstt{n}_{g}")
                bank_free[b] = f"Vstt{n}_{g}"
                g += 1
        if n == NIB - 1:
            for g in range(4):
                isub, ot = g // 2, g % 2
                sc.add("gpsimd",
                       lambda e, n=n, isub=isub, ot=ot: e.dma_start(
                           out=dout[n * IB + isub * 128:n * IB + (isub + 1) * 128,
                                    ot * 512:(ot + 1) * 512].rearrange(
                                        "(a p) h -> p a h", p=128),
                           in_=partd[:, isub:isub + 1, ot * 512:(ot + 1) * 512]),
                       deps=[f"Vstt{n}_{g}"],
                       event=f"Gout{n}" if g == 3 else None, dma=True)
        else:
            sc.add("gpsimd",
                   lambda e, n=n: e.dma_start(
                       out=dout[n * IB:(n + 1) * IB, :].rearrange("(a p) h -> p a h", p=128),
                       in_=partd[:, :, :]),
                   deps=[f"Vstt{n}_3"], event=f"Gout{n}", dma=True)

    sc.add("sync", lambda e: e.nop(), deps=[f"Gout{NIB-1}"])

    sc.resolve()

    @block.sync
    def _(e):
        sc.emit("sync", e, sems)

    @block.gpsimd
    def _(e):
        sc.emit("gpsimd", e, sems)

    @block.vector
    def _(e):
        sc.emit("vector", e, sems)

    @block.scalar
    def _(e):
        sc.emit("scalar", e, sems)

    @block.tensor
    def _(e):
        sc.emit("tensor", e, sems)

    st.close()
    return nc


def kernel(L0_attn_x, L0_attn_W_qkv, L0_attn_W_out, L0_ffn_W_r, L0_ffn_W_k,
           L0_ffn_W_v, L0_ffn_mu_r, L0_ffn_mu_k):
    x = np.asarray(L0_attn_x, np.float32)
    Wqkv = np.asarray(L0_attn_W_qkv, np.float32)
    ins = dict(
        WqT=np.ascontiguousarray(Wqkv[0::3].T),
        WkT=np.ascontiguousarray(Wqkv[1::3].T),
        WvT=np.ascontiguousarray(Wqkv[2::3].T),
        WoT=np.ascontiguousarray(np.asarray(L0_attn_W_out, np.float32).T),
        WrT=np.ascontiguousarray(np.asarray(L0_ffn_W_r, np.float32).T),
        WkfT=np.ascontiguousarray(np.asarray(L0_ffn_W_k, np.float32).T),
        WvfT=np.ascontiguousarray(np.asarray(L0_ffn_W_v, np.float32).T),
        mur=np.ascontiguousarray(np.asarray(L0_ffn_mu_r, np.float32).reshape(NCH, 128).T),
        muk=np.ascontiguousarray(np.asarray(L0_ffn_mu_k, np.float32).reshape(NCH, 128).T),
        ident=np.eye(128, dtype=np.float32),
    )
    nc = build_program()
    in_maps = [dict(x=np.ascontiguousarray(x[b]), **ins) for b in range(8)]
    res = run_bass_kernel_spmd(nc, in_maps, core_ids=list(range(8)))
    return np.stack([np.asarray(r["out"], np.float32) for r in res.results])
